# version 03
# baseline (speedup 1.0000x reference)
import numpy as np
import concourse.bass as bass
import concourse.mybir as mybir
from concourse.bass_utils import run_bass_kernel_spmd

F32 = mybir.dt.float32
BF16 = mybir.dt.bfloat16
AF = mybir.ActivationFunctionType
OP = mybir.AluOpType
AX = mybir.AxisListType

NB = 2
T = 2048
C = 1024
NCH = 8
TG = 512
NG = T // TG
NSUB = TG // 128
NH = 16
HD = 64
LAT = 512
LCH = LAT // 128
EPS = 1e-5


class _Dummy:
    def __getattr__(self, name):
        return lambda *a, **k: None


def build_nc(mode="full"):
    nc = bass.Bass("TRN2", target_bir_lowering=False, dynamic_dma_scratch_size=2048,
                   detect_race_conditions=False)

    x_h = nc.dram_tensor("x", [NB, C, T], F32, kind="ExternalInput").ap()
    wr_h = nc.dram_tensor("wr", [C, C], F32, kind="ExternalInput").ap()
    wk_h = nc.dram_tensor("wk", [C, C], F32, kind="ExternalInput").ap()
    wv_h = nc.dram_tensor("wv", [C, C], F32, kind="ExternalInput").ap()
    wdn_h = nc.dram_tensor("wdn", [C, LAT], F32, kind="ExternalInput").ap()
    wqu_h = nc.dram_tensor("wqu", [LAT, C], F32, kind="ExternalInput").ap()
    wku_h = nc.dram_tensor("wku", [LAT, 192], F32, kind="ExternalInput").ap()
    wpr_h = nc.dram_tensor("wpr", [64, NH * C], F32, kind="ExternalInput").ap()
    mur_h = nc.dram_tensor("mur", [128, NCH], F32, kind="ExternalInput").ap()
    muk_h = nc.dram_tensor("muk", [128, NCH], F32, kind="ExternalInput").ap()
    out_h = nc.dram_tensor("out", [NB, C, T], F32, kind="ExternalOutput").ap()
    bnc_h = nc.dram_tensor("bnc", [256], F32, kind="Internal").ap()

    from contextlib import ExitStack
    with ExitStack() as stk:
        def sb(name, shape, dt):
            return stk.enter_context(nc.sbuf_tensor(name, shape, dt))

        wr_sb = sb("wr_sb", [128, NCH * C], BF16)
        wk_sb = sb("wk_sb", [128, NCH * C], BF16)
        wv_sb = sb("wv_sb", [128, NCH * C], BF16)
        wdn_sb = sb("wdn_sb", [128, NCH * LAT], BF16)
        wqu_sb = sb("wqu_sb", [128, LCH * C], BF16)
        wku_sb = sb("wku_sb", [128, LCH * 192], BF16)
        wpr_sb = sb("wpr_sb", [64, NH * C], BF16)
        mur_sb = sb("mur_sb", [128, NCH], F32)
        muk_sb = sb("muk_sb", [128, NCH], F32)
        x_sb = sb("x_sb", [128, NCH * TG], F32)
        y_sb = sb("y_sb", [128, NCH * (TG + 1)], F32)
        d_sb = sb("d_sb", [128, NCH * TG], F32)
        xr_sb = sb("xr_sb", [128, NCH * TG], BF16)
        xk_sb = sb("xk_sb", [128, NCH * TG], BF16)
        kact_sb = sb("kact_sb", [128, NCH * TG], BF16)
        r_sb = sb("r_sb", [128, NCH * TG], BF16)
        lat_sb = sb("lat_sb", [128, LCH * TG], BF16)
        qT_sb = sb("qT_sb", [128, NCH * TG], BF16)
        kT_sb = sb("kT_sb", [128, T], BF16)
        vstage = sb("vstage", [128, TG], BF16)
        v_sb = sb("v_sb", [128, 16 * 65], BF16)
        probsT = sb("probsT", [128, 2 * 1024], BF16)
        ynorm = sb("ynorm", [128, NH * TG], BF16)
        amx = sb("amx", [128, 2], F32)
        amxT = sb("amxT", [1, 256], F32)
        cst = sb("cst", [128, 2], F32)
        biasT = sb("biasT", [128, 1], F32)
        carry = sb("carry", [128, 2 * NCH], F32)
        ones_bf = sb("ones_bf", [128, 1], BF16)
        ones64 = sb("ones64", [128, 64], BF16)
        onesrf = sb("onesrf", [1, 128], F32)
        id128 = sb("id128", [128, 128], BF16)
        mask8 = sb("mask8", [128, 1024], BF16)
        psA = stk.enter_context(nc.psum_tensor("psA", [128, 1024], F32))
        psB = stk.enter_context(nc.psum_tensor("psB", [128, 1024], F32))
        psY0 = stk.enter_context(nc.psum_tensor("psY0", [128, 1024], F32))
        psY1 = stk.enter_context(nc.psum_tensor("psY1", [128, 1024], F32))

        stage = ynorm[:, 0:4096].bitcast(F32)
        d_bf = d_sb[:, :].bitcast(BF16)
        rowsv = lat_sb[0:1, 0:2048].bitcast(F32)
        rowsA = rowsv[:, 0:512]
        rowsB = rowsv[:, 512:1024]

        S = {}
        for s in ("qx", "qs", "qg", "qa", "p", "v", "a", "g"):
            S[s] = stk.enter_context(nc.semaphore(s))

        block = stk.enter_context(nc.Block())

        ROT = [(psY0, 0, 4), (psY0, 512, 5), (psY1, 0, 6), (psY1, 512, 7)]

        def emit(me, eng):
            E = {n: (eng if me == n else _Dummy())
                 for n in ("sync", "gpsimd", "vector", "scalar", "tensor")}
            SY, GP, VE, AC, TE = (E["sync"], E["gpsimd"], E["vector"],
                                  E["scalar"], E["tensor"])
            cnt = {k: 0 for k in S}

            def inc(sem, inst, n=1):
                cnt[sem] += n
                if inst is not None:
                    inst.then_inc(S[sem], n)
                return cnt[sem]

            def W(engname, sem, val):
                if me == engname and val > 0:
                    eng.wait_ge(S[sem], val)

            bank_guard = {i: None for i in range(8)}

            def pe_use(banks):
                for b_ in banks:
                    gd = bank_guard[b_]
                    if gd:
                        W("tensor", gd[0], gd[1])

            def bank_rel(banks, sem):
                for b_ in banks:
                    bank_guard[b_] = (sem, cnt[sem])

            rot_i = [0]

            def rot_next():
                t, off, bank = ROT[rot_i[0] % 4]
                rot_i[0] += 1
                return t, off, bank

            buf = {}

            def buf_w(engname, name):
                gd = buf.get(name)
                if gd:
                    W(engname, gd[0], gd[1])

            def buf_r(name, sem):
                buf[name] = (sem, cnt[sem])

            def xseg(j):
                return x_sb[:, j * TG:(j + 1) * TG]

            def yseg(j):
                return y_sb[:, j * (TG + 1) + 1: (j + 1) * (TG + 1)]

            def ypseg(j):
                return y_sb[:, j * (TG + 1): j * (TG + 1) + TG]

            def dseg(j):
                return d_sb[:, j * TG:(j + 1) * TG]

            def seg(t, j, w=TG):
                return t[:, j * w:(j + 1) * w]

            GP.memset(ones_bf[:, :], 1.0)
            GP.memset(ones64[:, :], 1.0)
            GP.memset(onesrf[:, :], 1.0)
            GP.memset(cst[:, 0:1], EPS)
            GP.memset(cst[:, 1:2], 1.0)
            ig0 = GP.memset(id128[:, :], 1.0)
            ig1 = GP.memset(mask8[:, :], 1.0)
            inc("g", ig1, 1)
            W("gpsimd", "g", cnt["g"])
            GP.affine_select(id128[:, :], id128[:, :], [[1, 128]], OP.is_equal,
                             0.0, base=0, channel_multiplier=-1)
            GP.affine_select(mask8[:, :], mask8[:, :], [[0, 8], [1, 128]],
                             OP.is_ge, 0.0, base=0, channel_multiplier=-1)
            iv = GP.memset(v_sb[:, :].rearrange("p (j c) -> p j c", c=65)[:, :, 64:65], 1.0)
            inc("g", iv, 1)
            consts_g = cnt["g"]

            if me == "scalar":
                from concourse.hw_specs import get_activation_tables
                tabs = get_activation_tables(nc.m.arch)
                idx = list(tabs).index("natural_log_exp_and_others")
                eng.add_instruction(mybir.InstLoadActFuncSet(
                    name=nc.get_next_instruction_name(),
                    act_func_set_id=idx,
                    engine=mybir.EngineType.Activation))

            i1 = SY.dma_start(out=mur_sb[:, :], in_=mur_h[:, :])
            inc("qx", i1, 16)
            i1 = SY.dma_start(out=muk_sb[:, :], in_=muk_h[:, :])
            inc("qx", i1, 16)
            mu_qx = cnt["qx"]

            ld_cnt = {}
            def emit_loads(b_, g_):
                for j in range(NCH):
                    gd = buf.get(("xfree", j))
                    if gd:
                        W("sync", gd[0], gd[1])
                    ii = SY.dma_start(out=xseg(j),
                                      in_=x_h[b_, j * 128:(j + 1) * 128,
                                              g_ * TG:(g_ + 1) * TG])
                    ld_cnt[(b_, g_, j)] = inc("qx", ii, 16)
            emit_loads(0, 0)

            stage_free = [None, None]
            wt_ready = {}

            def load_weight(name, hbm, dst, nchunks, rows_, width, dst_seg_w):
                for ci in range(nchunks):
                    par = ci % 2
                    if stage_free[par] is not None:
                        W("scalar", "v", stage_free[par])
                    ii = AC.dma_start(out=stage[0:rows_, par * 1024: par * 1024 + width],
                                      in_=hbm[ci * rows_:(ci + 1) * rows_, 0:width])
                    qg_v = inc("qg", ii, 16)
                    W("vector", "qg", qg_v)
                    iv_ = VE.tensor_copy(dst[0:rows_, ci * dst_seg_w: ci * dst_seg_w + width],
                                         stage[0:rows_, par * 1024: par * 1024 + width])
                    stage_free[par] = inc("v", iv_)
                wt_ready[name] = cnt["v"]

            load_weight("wr", wr_h, wr_sb, NCH, 128, C, C)
            load_weight("wk", wk_h, wk_sb, NCH, 128, C, C)
            load_weight("wv", wv_h, wv_sb, NCH, 128, C, C)
            load_weight("wdn", wdn_h, wdn_sb, NCH, 128, LAT, LAT)
            load_weight("wqu", wqu_h, wqu_sb, LCH, 128, C, C)
            load_weight("wku", wku_h, wku_sb, LCH, 128, 192, 192)

            for ci in range(16):
                par = ci % 2
                if stage_free[par] is not None:
                    W("scalar", "v", stage_free[par])
                ii = AC.dma_start(out=stage[0:64, par * 1024:(par + 1) * 1024],
                                  in_=wpr_h[0:64, ci * 1024:(ci + 1) * 1024])
                qg_v = inc("qg", ii, 16)
                W("vector", "qg", qg_v)
                iv_ = VE.tensor_copy(wpr_sb[0:64, ci * 1024:(ci + 1) * 1024],
                                     stage[0:64, par * 1024:(par + 1) * 1024])
                stage_free[par] = inc("v", iv_)
            wt_ready["wpr"] = cnt["v"]

            def LN(b_, g_, dst):
                cast_done = []
                sq_done = []
                buf_w("vector", "xr")
                buf_w("vector", "xk")
                for j in range(NCH):
                    if dst == "y" and (b_, g_, j) in ld_cnt:
                        W("vector", "qx", ld_cnt[(b_, g_, j)])
                    iv_ = VE.tensor_copy(seg(xr_sb, j), xseg(j))
                    cast_done.append(inc("v", iv_))
                for j in range(NCH):
                    iv_ = VE.tensor_mul(seg(xk_sb, j), seg(xr_sb, j), seg(xr_sb, j))
                    sq_done.append(inc("v", iv_))
                pe_use([0, 2])
                W("tensor", "g", consts_g)
                last = None
                for j in range(NCH):
                    W("tensor", "v", cast_done[j])
                    TE.matmul(psA[0:1, 0:TG], ones_bf[:, :], seg(xr_sb, j),
                              start=(j == 0), stop=(j == NCH - 1))
                    W("tensor", "v", sq_done[j])
                    last = TE.matmul(psB[0:1, 0:TG], ones_bf[:, :], seg(xk_sb, j),
                                     start=(j == 0), stop=(j == NCH - 1))
                stats_p = inc("p", last)
                W("vector", "p", stats_p)
                VE.tensor_scalar(out=rowsA, in0=psA[0:1, 0:TG],
                                 scalar1=1.0 / C, scalar2=None, op0=OP.mult)
                VE.tensor_mul(rowsB, rowsA, rowsA)
                iv_ = VE.scalar_tensor_tensor(out=rowsB, in0=psB[0:1, 0:TG],
                                              scalar=1.0 / C, in1=rowsB,
                                              op0=OP.mult, op1=OP.subtract)
                var_v = inc("v", iv_)
                bank_rel([0, 2], "v")
                W("scalar", "v", var_v)
                AC.activation(rowsB, rowsB, AF.Ln, bias=cst[0:1, 0:1])
                ia = AC.activation(rowsB, rowsB, AF.Exp, scale=-0.5)
                rstd_a = inc("a", ia)
                W("vector", "a", rstd_a)
                iv_ = VE.tensor_mul(rowsA, rowsA, rowsB)
                mrs_v = inc("v", iv_)
                pe_use([1, 3])
                W("tensor", "a", rstd_a)
                W("tensor", "v", mrs_v)
                TE.matmul(psA[:, 512:1024], onesrf[0:1, 0:128], rowsB,
                          start=True, stop=True)
                ip = TE.matmul(psB[:, 512:1024], onesrf[0:1, 0:128], rowsA,
                               start=True, stop=True)
                bc_p = inc("p", ip)
                apply_done = []
                W("vector", "p", bc_p)
                buf_w("vector", "d")
                for j in range(NCH):
                    if dst == "y":
                        gd = buf.get(("ystore", j))
                        if gd:
                            W("vector", gd[0], gd[1])
                    VE.tensor_mul(dseg(j), xseg(j), psA[:, 512:1024])
                    dst_ap = yseg(j) if dst == "y" else seg(xr_sb, j)
                    iv_ = VE.scalar_tensor_tensor(out=dst_ap, in0=dseg(j), scalar=1.0,
                                                  in1=psB[:, 512:1024],
                                                  op0=OP.mult, op1=OP.subtract)
                    apply_done.append(inc("v", iv_))
                bank_rel([1, 3], "v")
                return apply_done

            def FFN(b_, g_, out_mode):
                cb = 0 if out_mode == "x" else NCH
                W("vector", "qx", mu_qx)
                ycar = y_sb[:, :].rearrange("p (j c) -> p j c", c=TG + 1)
                VE.tensor_copy(ycar[:, :, 0:1], carry[:, cb:cb + NCH].unsqueeze(2))
                VE.tensor_copy(carry[:, cb:cb + NCH].unsqueeze(2),
                               ycar[:, :, TG:TG + 1])
                xr_done = []
                xk_done = []
                buf_w("vector", "xr")
                buf_w("vector", "xk")
                buf_w("vector", "d")
                for j in range(NCH):
                    VE.tensor_sub(dseg(j), yseg(j), ypseg(j))
                    iv_ = VE.scalar_tensor_tensor(out=seg(xr_sb, j), in0=dseg(j),
                                                  scalar=mur_sb[:, j:j + 1],
                                                  in1=ypseg(j), op0=OP.mult, op1=OP.add)
                    xr_done.append(inc("v", iv_))
                    iv_ = VE.scalar_tensor_tensor(out=seg(xk_sb, j), in0=dseg(j),
                                                  scalar=muk_sb[:, j:j + 1],
                                                  in1=ypseg(j), op0=OP.mult, op1=OP.add)
                    xk_done.append(inc("v", iv_))
                r_done = [None] * NCH
                u_done = [None] * NCH
                rot_of = {}
                for dq in range(NCH):
                    t_, off, bank = rot_next()
                    rot_of[dq] = (t_, off, bank)
                    pe_use([bank])
                    W("tensor", "v", xr_done[-1])
                    W("tensor", "v", wt_ready["wr"])
                    last = None
                    for j in range(NCH):
                        last = TE.matmul(t_[:, off:off + TG],
                                         wr_sb[:, j * C + dq * 128: j * C + (dq + 1) * 128],
                                         seg(xr_sb, j),
                                         start=(j == 0), stop=(j == NCH - 1))
                    mm_p = inc("p", last)
                    W("scalar", "p", mm_p)
                    ia = AC.activation(yseg(dq), t_[:, off:off + TG], AF.Exp, scale=-1.0)
                    u_done[dq] = inc("a", ia)
                    bank_rel([bank], "a")
                    AC.activation(dseg(dq), yseg(dq), AF.Ln, bias=cst[:, 1:2])
                    ia = AC.activation(seg(r_sb, dq), dseg(dq), AF.Exp, scale=-1.0)
                    r_done[dq] = inc("a", ia)
                kact_done = [None] * NCH
                for dq in range(NCH):
                    t_, off, bank = rot_next()
                    pe_use([bank])
                    W("tensor", "v", xk_done[-1])
                    W("tensor", "v", wt_ready["wk"])
                    last = None
                    for j in range(NCH):
                        last = TE.matmul(t_[:, off:off + TG],
                                         wk_sb[:, j * C + dq * 128: j * C + (dq + 1) * 128],
                                         seg(xk_sb, j),
                                         start=(j == 0), stop=(j == NCH - 1))
                    mm_p = inc("p", last)
                    W("vector", "p", mm_p)
                    W("vector", "a", r_done[dq])
                    iv_ = VE.tensor_scalar(out=dseg(dq), in0=t_[:, off:off + TG],
                                           scalar1=0.0, scalar2=None, op0=OP.max)
                    rl_v = inc("v", iv_)
                    bank_rel([bank], "v")
                    W("scalar", "v", rl_v)
                    buf_w("scalar", ("kact", dq))
                    ia = AC.activation(seg(kact_sb, dq), dseg(dq), AF.Square)
                    kact_done[dq] = inc("a", ia)
                out_done = [None] * NCH
                for dq in range(NCH):
                    t_, off, bank = rot_next()
                    pe_use([bank])
                    W("tensor", "a", kact_done[-1])
                    W("tensor", "v", wt_ready["wv"])
                    last = None
                    for j in range(NCH):
                        last = TE.matmul(t_[:, off:off + TG],
                                         wv_sb[:, j * C + dq * 128: j * C + (dq + 1) * 128],
                                         seg(kact_sb, j),
                                         start=(j == 0), stop=(j == NCH - 1))
                    mm_p = inc("p", last)
                    buf_r(("kact", dq), "p")
                    W("vector", "p", mm_p)
                    iv_ = VE.scalar_tensor_tensor(out=dseg(dq), in0=t_[:, off:off + TG],
                                                  scalar=0.5, in1=seg(r_sb, dq),
                                                  op0=OP.mult, op1=OP.mult)
                    inc("v", iv_)
                    bank_rel([bank], "v")
                    if out_mode == "x":
                        iv_ = VE.tensor_add(xseg(dq), xseg(dq), dseg(dq))
                        out_done[dq] = inc("v", iv_)
                    else:
                        iv_ = VE.tensor_add(yseg(dq), xseg(dq), dseg(dq))
                        out_done[dq] = inc("v", iv_)
                        buf[("xfree", dq)] = ("v", cnt["v"])
                        W("sync", "v", out_done[dq])
                        ii = SY.dma_start(out=out_h[b_, dq * 128:(dq + 1) * 128,
                                                    g_ * TG:(g_ + 1) * TG],
                                          in_=yseg(dq))
                        st_v = inc("qs", ii, 16)
                        buf[("ystore", dq)] = ("qs", st_v)
                buf_r("d", "v")
                return out_done

            def ATTN_PRE(b_, g_, z_done):
                lat_done = []
                for lc in range(LCH):
                    t_, off, bank = rot_next()
                    pe_use([bank])
                    W("tensor", "v", z_done[-1])
                    W("tensor", "v", wt_ready["wdn"])
                    last = None
                    for j in range(NCH):
                        last = TE.matmul(t_[:, off:off + TG],
                                         wdn_sb[:, j * LAT + lc * 128: j * LAT + (lc + 1) * 128],
                                         seg(xr_sb, j),
                                         start=(j == 0), stop=(j == NCH - 1))
                    mm_p = inc("p", last)
                    W("vector", "p", mm_p)
                    buf_w("vector", "lat")
                    iv_ = VE.tensor_copy(seg(lat_sb, lc), t_[:, off:off + TG])
                    lat_done.append(inc("v", iv_))
                    bank_rel([bank], "v")
                t_, off, bank = rot_next()
                pe_use([bank])
                W("tensor", "v", lat_done[-1])
                W("tensor", "v", wt_ready["wku"])
                last = None
                for lc in range(LCH):
                    last = TE.matmul(t_[:, off:off + TG],
                                     wku_sb[:, lc * 192: lc * 192 + 128],
                                     seg(lat_sb, lc),
                                     start=(lc == 0), stop=(lc == LCH - 1))
                kv1_p = inc("p", last)
                W("vector", "p", kv1_p)
                buf_w("vector", "kT")
                iv_ = VE.tensor_copy(kT_sb[:, g_ * TG:(g_ + 1) * TG], t_[:, off:off + TG])
                kT_v = inc("v", iv_)
                bank_rel([bank], "v")
                t2_, off2, bank2 = rot_next()
                pe_use([bank2])
                last = None
                for lc in range(LCH):
                    last = TE.matmul(t2_[0:64, off2:off2 + TG],
                                     wku_sb[:, lc * 192 + 128: lc * 192 + 192],
                                     seg(lat_sb, lc),
                                     start=(lc == 0), stop=(lc == LCH - 1))
                kv2_p = inc("p", last)
                W("vector", "p", kv2_p)
                iv_ = VE.tensor_copy(vstage[0:64, 0:TG], t2_[0:64, off2:off2 + TG])
                vst_v = inc("v", iv_)
                bank_rel([bank2], "v")
                t3_, off3, bank3 = rot_next()
                pe_use([bank3])
                W("tensor", "v", vst_v)
                t3bf = t3_[:, off3:off3 + 512].bitcast(BF16)
                last = None
                for sub in range(NSUB):
                    last = TE.transpose(t3bf[:, sub * 64:(sub + 1) * 64],
                                        vstage[0:64, sub * 128:(sub + 1) * 128],
                                        id128[0:64, 0:64])
                tr_p = inc("p", last)
                W("vector", "p", tr_p)
                buf_w("vector", "v_sb")
                vv = v_sb[:, :].rearrange("p (j c) -> p j c", c=65)
                iv_ = VE.tensor_copy(vv[:, g_ * NSUB:(g_ + 1) * NSUB, 0:64],
                                     t3bf[:, 0:256].rearrange("p (s c) -> p s c", c=64))
                v_ready = inc("v", iv_)
                bank_rel([bank3], "v")
                qT_done = []
                for dq in range(NCH):
                    t_, off, bank = rot_next()
                    pe_use([bank])
                    W("tensor", "v", wt_ready["wqu"])
                    last = None
                    for lc in range(LCH):
                        last = TE.matmul(t_[:, off:off + TG],
                                         wqu_sb[:, lc * C + dq * 128: lc * C + (dq + 1) * 128],
                                         seg(lat_sb, lc),
                                         start=(lc == 0), stop=(lc == LCH - 1))
                    mm_p = inc("p", last)
                    W("vector", "p", mm_p)
                    buf_w("vector", "qT")
                    iv_ = VE.tensor_scalar(out=seg(qT_sb, dq), in0=t_[:, off:off + TG],
                                           scalar1=0.125, scalar2=None, op0=OP.mult)
                    qT_done.append(inc("v", iv_))
                    bank_rel([bank], "v")
                buf_w("vector", "amx")
                iv_ = VE.reduce_max(amx[:, 0:1], qT_sb[:, :], axis=AX.X,
                                    apply_absolute_value=True)
                iv_ = VE.reduce_max(amx[0:64, 1:2], kT_sb[0:64, 0:(g_ + 1) * TG],
                                    axis=AX.X, apply_absolute_value=True)
                amax_v = inc("v", iv_)
                W("scalar", "v", amax_v)
                ia = AC.dma_start(out=bnc_h[0:128], in_=amx[:, 0:1])
                inc("qa", ia, 16)
                ia = AC.dma_start(out=bnc_h[128:192], in_=amx[0:64, 1:2])
                qa1 = inc("qa", ia, 16)
                buf_r("amx", "qa")
                W("scalar", "qa", qa1)
                ia = AC.dma_start(out=amxT[0:1, 0:192], in_=bnc_h[0:192])
                qa2 = inc("qa", ia, 16)
                W("vector", "qa", qa2)
                VE.reduce_max(amxT[0:1, 192:193], amxT[0:1, 0:128], axis=AX.X)
                VE.reduce_max(amxT[0:1, 193:194], amxT[0:1, 128:192], axis=AX.X)
                iv_ = VE.scalar_tensor_tensor(out=amxT[0:1, 195:196],
                                              in0=amxT[0:1, 192:193], scalar=-64.0,
                                              in1=amxT[0:1, 193:194],
                                              op0=OP.mult, op1=OP.mult)
                nb_v = inc("v", iv_)
                t_, off, bank = rot_next()
                pe_use([bank])
                W("tensor", "v", nb_v)
                ip = TE.matmul(t_[:, off:off + 1], onesrf[0:1, 0:128],
                               amxT[0:1, 195:196], start=True, stop=True)
                bb_p = inc("p", ip)
                W("vector", "p", bb_p)
                iv_ = VE.tensor_copy(biasT[:, 0:1], t_[:, off:off + 1])
                bias_v = inc("v", iv_)
                bank_rel([bank], "v")
                return kT_v, v_ready, qT_done[-1], bias_v

            def ATTN(b_, g_, sub, kv_ready, qt_v, bias_v, probst_guard, level=3):
                a_ = g_ * NSUB + sub
                nj = a_ + 1
                steps = [(j, h) for j in range(nj) for h in (0, 1)]
                scores_p = {}
                exp_a = {}
                mask_g = {}
                attv_p = {}

                def emit_scores(idx):
                    j, h = steps[idx]
                    par = idx % 2
                    sc_t = psA if par == 0 else psB
                    banks = [0, 1] if par == 0 else [2, 3]
                    pe_use(banks)
                    kTv, vv_ = kv_ready[j]
                    W("tensor", "v", kTv)
                    W("tensor", "v", qt_v)
                    for i, hh in enumerate((0, 2, 4, 6, 1, 3, 5, 7)):
                        hgl = h * 8 + hh
                        base = (hgl % 2) * 64
                        ii = TE.matmul(
                            sc_t[:, i * 128:(i + 1) * 128],
                            kT_sb[base:base + 64, j * 128:(j + 1) * 128],
                            qT_sb[base:base + 64,
                                  (hgl // 2) * TG + sub * 128:
                                  (hgl // 2) * TG + (sub + 1) * 128],
                            start=(i in (0, 4)), stop=(i in (3, 7)))
                    scores_p[idx] = inc("p", ii)

                def emit_exp(idx):
                    j, h = steps[idx]
                    par = idx % 2
                    sc_t = psA if par == 0 else psB
                    banks = [0, 1] if par == 0 else [2, 3]
                    W("scalar", "p", scores_p[idx])
                    W("scalar", "v", bias_v)
                    pg = probst_guard[par]
                    if pg:
                        W("scalar", pg[0], pg[1])
                    ia = AC.activation(probsT[:, par * 1024:(par + 1) * 1024],
                                       sc_t[:, 0:1024], AF.Exp,
                                       bias=biasT[:, 0:1], scale=1.0)
                    exp_a[idx] = inc("a", ia)
                    bank_rel(banks, "a")
                    if j == a_ and level >= 1:
                        W("gpsimd", "a", exp_a[idx])
                        W("gpsimd", "g", consts_g)
                        ig = GP.tensor_mul(probsT[:, par * 1024:(par + 1) * 1024],
                                           probsT[:, par * 1024:(par + 1) * 1024],
                                           mask8[:, :])
                        mask_g[idx] = inc("g", ig)

                def emit_attv(idx):
                    j, h = steps[idx]
                    par = idx % 2
                    yt = psY0 if h == 0 else psY1
                    ybanks = [4, 5] if h == 0 else [6, 7]
                    if j == 0:
                        pe_use(ybanks)
                    W("tensor", "a", exp_a[idx])
                    if idx in mask_g:
                        W("tensor", "g", mask_g[idx])
                    kTv, vv_ = kv_ready[j]
                    W("tensor", "v", vv_)
                    TE.matmul(yt[0:65, 0:512], v_sb[:, j * 65: j * 65 + 65],
                              probsT[:, par * 1024: par * 1024 + 512],
                              start=(j == 0), stop=(j == nj - 1))
                    ii = TE.matmul(yt[0:65, 512:1024], v_sb[:, j * 65: j * 65 + 65],
                                   probsT[:, par * 1024 + 512: (par + 1) * 1024],
                                   start=(j == 0), stop=(j == nj - 1))
                    attv_p[idx] = inc("p", ii)
                    probst_guard[par] = ("p", cnt["p"])

                for idx in range(len(steps)):
                    emit_scores(idx)
                    if level >= 2 and idx >= 2:
                        emit_attv(idx - 2)
                    emit_exp(idx)
                if level >= 2:
                    if len(steps) >= 2:
                        emit_attv(len(steps) - 2)
                    emit_attv(len(steps) - 1)
                if level < 3:
                    return cnt["v"]

                for h in (0, 1):
                    yt = psY0 if h == 0 else psY1
                    ybanks = [4, 5] if h == 0 else [6, 7]
                    last_idx = 2 * (nj - 1) + h
                    W("vector", "p", attv_p[last_idx])
                    yraw = d_sb[:, 1024:2048]
                    iv_ = VE.tensor_copy(yraw[0:65, 0:1024], yt[0:65, 0:1024])
                    ev_v = inc("v", iv_)
                    bank_rel(ybanks, "v")
                    W("scalar", "v", ev_v)
                    AC.activation(d_sb[64:65, 0:1024], yraw[64:65, 0:1024], AF.Ln)
                    ia = AC.activation(d_bf[64:65, 4096:5120], d_sb[64:65, 0:1024],
                                       AF.Exp, scale=-1.0)
                    inv_a = inc("a", ia)
                    for p2 in range(2):
                        bank = ybanks[p2]
                        pe_use([bank])
                        W("tensor", "a", inv_a)
                        W("tensor", "g", consts_g)
                        ip = TE.matmul(yt[0:64, p2 * 512:(p2 + 1) * 512],
                                       ones64[64:65, 0:64],
                                       d_bf[64:65, 4096 + p2 * 512: 4096 + (p2 + 1) * 512],
                                       start=True, stop=True)
                        ib_p = inc("p", ip)
                        W("vector", "p", ib_p)
                        yn = ynorm[0:64, :].rearrange("p (hp ho t) -> p hp ho t",
                                                      ho=2, t=TG)
                        dst = yn[:, h * 4:(h + 1) * 4, p2:p2 + 1,
                                 sub * 128:(sub + 1) * 128]
                        src0 = yraw[0:64, p2 * 512:(p2 + 1) * 512].rearrange(
                            "p (h t) -> p h t", t=128)
                        src1 = yt[0:64, p2 * 512:(p2 + 1) * 512].rearrange(
                            "p (h t) -> p h t", t=128)
                        iv_ = VE.scalar_tensor_tensor(out=dst, in0=src0, scalar=1.0,
                                                      in1=src1, op0=OP.mult,
                                                      op1=OP.mult)
                        yn_v = inc("v", iv_)
                        bank_rel([bank], "v")
                return cnt["v"]

            def PROJ(b_, g_, ynorm_v):
                xupd = []
                for dq in range(NCH):
                    t_, off, bank = rot_next()
                    pe_use([bank])
                    W("tensor", "v", ynorm_v)
                    W("tensor", "v", wt_ready["wpr"])
                    last = None
                    for h in range(NH):
                        last = TE.matmul(t_[:, off:off + TG],
                                         wpr_sb[0:64, h * C + dq * 128: h * C + (dq + 1) * 128],
                                         ynorm[0:64, h * TG:(h + 1) * TG],
                                         start=(h == 0), stop=(h == NH - 1))
                    mm_p = inc("p", last)
                    W("vector", "p", mm_p)
                    iv_ = VE.tensor_add(xseg(dq), xseg(dq), t_[:, off:off + TG])
                    xupd.append(inc("v", iv_))
                    bank_rel([bank], "v")
                buf_r("ynorm", "p")
                return xupd

            def store_x(b_, g_, waits):
                for dq in range(NCH):
                    sem_, val_ = waits[dq]
                    W("sync", sem_, val_)
                    ii = SY.dma_start(out=out_h[b_, dq * 128:(dq + 1) * 128,
                                                g_ * TG:(g_ + 1) * TG],
                                      in_=xseg(dq))
                    st_v = inc("qs", ii, 16)
                    buf[("xfree", dq)] = ("qs", st_v)

            for b_ in range(NB):
                iv_ = VE.memset(carry[:, :], 0.0)
                inc("v", iv_)
                kv_ready_all = {}
                probst_guard = [None, None]
                for g_ in range(NG):
                    if mode == "copy":
                        store_x(b_, g_, [("qx", ld_cnt[(b_, g_, j)])
                                         for j in range(NCH)])
                    else:
                        LN(b_, g_, "y")
                        out_done = FFN(b_, g_, "x")
                        buf_r("xr", "p")
                        buf_r("xk", "p")
                        if mode in ("pre", "attn0", "attn1", "attn2", "attn", "full"):
                            z_done = LN(b_, g_, "z")
                            kT_v, v_ready, qT_last, bias_v = ATTN_PRE(b_, g_, z_done)
                            for sub in range(NSUB):
                                kv_ready_all[g_ * NSUB + sub] = (kT_v, v_ready)
                            buf_r("xr", "p")
                            buf_r("lat", "p")
                        if mode in ("attn0", "attn1", "attn2", "attn", "full"):
                            level = {"attn0": 0, "attn1": 1, "attn2": 2}.get(mode, 3)
                            ynorm_v = None
                            buf_w("vector", "ynorm")
                            for sub in range(NSUB):
                                ynorm_v = ATTN(b_, g_, sub, kv_ready_all, qT_last,
                                               bias_v, probst_guard, level)
                            if level >= 3:
                                out_done = PROJ(b_, g_, ynorm_v)
                            buf_r("qT", "p")
                            buf_r("d", "p")
                        if mode in ("ffn2", "full"):
                            LN(b_, g_, "y")
                            FFN(b_, g_, "y")
                            buf_r("xr", "p")
                            buf_r("xk", "p")
                        else:
                            store_x(b_, g_, [("v", out_done[j])
                                             for j in range(NCH)])
                    if g_ + 1 < NG:
                        emit_loads(b_, g_ + 1)
                    elif b_ + 1 < NB:
                        emit_loads(b_ + 1, 0)
                buf_r("kT", "p")
                buf_r("v_sb", "p")
            W("sync", "qs", cnt["qs"])

        block.sync(lambda e: emit("sync", e))
        block.gpsimd(lambda e: emit("gpsimd", e))
        block.vector(lambda e: emit("vector", e))
        block.scalar(lambda e: emit("scalar", e))
        block.tensor(lambda e: emit("tensor", e))
    return nc


_NC = None


def _get_nc():
    global _NC
    if _NC is None:
        _NC = build_nc()
    return _NC


def kernel(**inputs):
    x = np.asarray(inputs["L0_attn_x"], np.float32)
    x_fm = np.ascontiguousarray(np.transpose(x, (0, 2, 1)))
    wr_t = np.ascontiguousarray(np.asarray(inputs["L0_ffn_W_r"], np.float32).T)
    wk_t = np.ascontiguousarray(np.asarray(inputs["L0_ffn_W_k"], np.float32).T)
    wv_t = np.ascontiguousarray(np.asarray(inputs["L0_ffn_W_v"], np.float32).T)
    wdn = np.ascontiguousarray(np.asarray(inputs["L0_attn_Wc_down"], np.float32))
    wqu = np.ascontiguousarray(np.asarray(inputs["L0_attn_Wq_up"], np.float32))
    wkv = np.asarray(inputs["L0_attn_Wkv_up"], np.float32)
    wku2 = np.ascontiguousarray(np.concatenate(
        [wkv[:, 0:64], wkv[:, 0:64], wkv[:, 64:128]], axis=1))
    wpr = np.asarray(inputs["L0_attn_Wc_proj"], np.float32)
    wpr2 = np.ascontiguousarray(
        wpr.reshape(NH, 64, C).transpose(1, 0, 2).reshape(64, NH * C))
    mur = np.ascontiguousarray(
        np.asarray(inputs["L0_ffn_mu_r"], np.float32).reshape(NCH, 128).T)
    muk = np.ascontiguousarray(
        np.asarray(inputs["L0_ffn_mu_k"], np.float32).reshape(NCH, 128).T)

    nc = _get_nc()
    in_maps = []
    for i in range(8):
        in_maps.append({
            "x": np.ascontiguousarray(x_fm[2 * i: 2 * i + 2]),
            "wr": wr_t, "wk": wk_t, "wv": wv_t,
            "wdn": wdn, "wqu": wqu, "wku": wku2, "wpr": wpr2,
            "mur": mur, "muk": muk,
        })
    res = run_bass_kernel_spmd(nc, in_maps, core_ids=list(range(8)))
    out_fm = np.concatenate([np.asarray(res.results[i]["out"]) for i in range(8)],
                            axis=0)
    return np.ascontiguousarray(np.transpose(out_fm, (0, 2, 1)).astype(np.float32))
